# version 05 (final)
# baseline (speedup 1.0000x reference)
import numpy as np
import ml_dtypes
from contextlib import ExitStack

import concourse.bass as bass
import concourse.mybir as mybir
from concourse.bass import IndirectOffsetOnAxis
from concourse.bass_utils import run_bass_kernel_spmd

F32 = mybir.dt.float32
BF16 = mybir.dt.bfloat16
I32 = mybir.dt.int32
AF = mybir.ActivationFunctionType
ALU = mybir.AluOpType
AX = mybir.AxisListType
BF = ml_dtypes.bfloat16

B, T, C = 2, 8192, 2048
DH = 256
NCC = 16
NT = 64
E, FM, FS = 8, 1408, 5632
NFM = 11
NFS = 44
OWN = 2048
NEG = -1.0e9
SCALE = DH ** -0.5


class Sem:
    __slots__ = ("h", "n")

    def __init__(self, nc, es, name):
        self.h = es.enter_context(nc.semaphore(name))
        self.n = 0

    def inc(self, instr, k=1):
        instr.then_inc(self.h, k)
        self.n += k
        return instr

    def dinc(self, instr):
        return self.inc(instr, 16)

    def wait(self, eng, n=None):
        n = self.n if n is None else n
        if n > 0:
            eng.wait_ge(self.h, n)


class G:
    pass


def build_nc(phases=4, dbg=False, feat=None):
    nc = bass.Bass("TRN2", target_bir_lowering=False, debug=False)
    g = G()
    g.nc = nc

    def din(name, shape, dt):
        return nc.dram_tensor(name, shape, dt, kind="ExternalInput").ap()

    g.x = din("x", [T, C], F32)
    g.x_own = din("x_own", [OWN, C], F32)
    g.wq = din("wq", [C, 2 * DH], F32)
    g.wk = din("wk", [C, DH], F32)
    g.wv = din("wv", [C, DH], F32)
    g.wo = din("wo", [C, C], F32)
    g.hs = din("hs", [1, 2], F32)
    g.gwsg_t = din("gwsg_t", [C, 16], F32)
    g.fc1_t = din("fc1_t", [E, C, FM], F32)
    g.fc2_t = din("fc2_t", [E, C, FM], F32)
    g.proj_t = din("proj_t", [E, FM, C], F32)
    g.sfc1_t = din("sfc1_t", [C, FS], F32)
    g.sfc2_t = din("sfc2_t", [C, FS], F32)
    g.sproj_t = din("sproj_t", [FS, C], F32)
    g.cos_t = din("cos_t", [128, T], F32)
    g.sin_t = din("sin_t", [128, T], F32)
    g.dirs_c = din("dirs_c", [3, 2], BF16)
    g.dirs_b = din("dirs_b", [3, 256], BF16)
    g.ident_bf = din("ident_bf", [128, 128], BF16)
    g.ident_f = din("ident_f", [128, 128], F32)
    g.ones_f = din("ones_f", [1, 128], F32)
    g.ones_bf = din("ones_bf", [128, 1], BF16)
    g.mask_f = din("mask_f", [128, 128], F32)
    g.rowidx = din("rowidx", [128, 64], I32)
    g.actc = din("actc", [128, 2], F32)

    g.out = nc.dram_tensor("out", [OWN, C], F32, kind="ExternalOutput").ap()

    sk = "ExternalOutput" if dbg else "Internal"
    g.dbg = dbg
    g.q_scr = nc.dram_tensor("q_scr", [2, 2, 128, T], BF16, kind=sk)
    g.xn_scr = nc.dram_tensor("xn_scr", [NCC, 128, OWN], BF16, kind=sk)
    g.ag_in = nc.dram_tensor("ag_in", [T, 2 * DH], BF16, kind="Internal")
    g.ag_out = nc.dram_tensor("ag_out", [4 * T, 2 * DH], BF16, kind="Internal")
    if dbg:
        g.dbg_gates = nc.dram_tensor("dbg_gates", [128, 144], F32, kind="ExternalOutput")
        g.dbg_agin = nc.dram_tensor("dbg_agin", [T, 2 * DH], BF16, kind="ExternalOutput")
        g.dbg_agout = nc.dram_tensor("dbg_agout", [4 * T, 2 * DH], BF16, kind="ExternalOutput")
    g.h_scr = nc.dram_tensor("h_scr", [E, NFM, 128, OWN], BF16, kind=sk)
    g.sh_scr = nc.dram_tensor("sh_scr", [NFS, 128, OWN], BF16, kind=sk)
    if dbg:
        g.dbg_kt = nc.dram_tensor("dbg_kt", [128, 2 * T], BF16, kind="ExternalOutput").ap()
        g.dbg_v = nc.dram_tensor("dbg_v", [128, NT * DH], BF16, kind="ExternalOutput").ap()
        g.dbg_log = nc.dram_tensor("dbg_log", [128, 16 * 16], F32, kind="ExternalOutput").ap()
        g.dbg_aux = nc.dram_tensor("dbg_aux", [128, 4 * NT], F32, kind="ExternalOutput").ap()

    g.feat = set(feat.split(",")) if feat else {"ln", "tp", "qkv", "aux", "own", "epi", "c4", "c5", "c6"}
    es = ExitStack()
    with es:
        g.es = es
        g.sems = {}

        def S(name):
            if name not in g.sems:
                g.sems[name] = Sem(nc, es, name)
            return g.sems[name]
        g.S = S

        def sb(name, shape, dt, stack=None):
            return (stack or es).enter_context(nc.sbuf_tensor(name, shape, dt))

        def ps(name, shape, dt, stack=None):
            return (stack or es).enter_context(nc.psum_tensor(name, shape, dt))
        g.sb, g.ps = sb, ps

        g.c_idbf = sb("c_idbf", [128, 128], BF16)
        g.c_idf = sb("c_idf", [128, 128], F32)
        g.c_onesf = sb("c_onesf", [1, 128], F32)
        g.c_onesbf = sb("c_onesbf", [128, 1], BF16)
        g.c_mask = sb("c_mask", [128, 128], F32)
        g.c_dirc = sb("c_dirc", [3, 2], BF16)
        g.c_dirb = sb("c_dirb", [3, 256], BF16)
        g.logits = sb("logits", [128, 16 * 16], F32)
        g.gates = sb("gates", [128, 16 * 8], F32)
        g.sg = sb("sg", [128, 16], F32)
        g.scol = sb("scol", [128, 2], F32)
        g.omscol = sb("omscol", [128, 2], F32)
        g.ridx = sb("ridx", [128, 64], I32)
        g.hs_sb = sb("hs_sb", [1, 2], F32)
        g.actc_sb = sb("actc_sb", [128, 2], F32)
        g.probs = sb("probs", [128, 8], F32)
        g.pmask = sb("pmask", [128, 8], F32)
        g.cmini = sb("cmini", [128, 8], F32)

        g.es_ab = ExitStack()
        g.kt = sb("kt", [128, 2 * T], BF16, g.es_ab)
        g.vsb = sb("vsb", [128, NT * DH], BF16, g.es_ab)
        g.qn2 = sb("qn2", [128, 2 * NT], F32, g.es_ab)
        g.qn = sb("qn", [128, 2 * NT], F32, g.es_ab)
        g.kn2t = sb("kn2t", [128, NT], F32, g.es_ab)
        g.tabA = sb("tabA", [128, NT], F32, g.es_ab)
        g.tabB = sb("tabB", [128, NT], F32, g.es_ab)
        g.kmaxb = sb("kmaxb", [128, NT], F32, g.es_ab)
        g.cmaxb = sb("cmaxb", [128, NT], F32, g.es_ab)
        g.cminb = sb("cminb", [128, NT], F32, g.es_ab)

        with nc.Block() as block:
            g.block = block
            emit_setup(g)
            emit_phaseA(g)
            if "epi" in g.feat:
                emit_phaseA_epi(g)
            emit_Abar(g)
            if dbg:
                emit_dbgA(g)
            if phases >= 2:
                emit_phaseB(g)
            if phases >= 3:
                emit_collective(g)
            if phases >= 4:
                emit_phaseC(g)
            else:
                emit_stub_out(g)
            if phases < 2:
                g.es_a.close()
            elif phases < 4:
                g.es_b.close()
            if phases < 4:
                g.es_ab.close()

    return nc


def emit_Abar(g):
    S, block = g.S, g.block

    def bar(q):
        q.wait_ge(S("sA_rope").h, S("sA_rope").n)
        q.wait_ge(S("sA_vev").h, S("sA_vev").n)
        q.wait_ge(S("sA_log").h, S("sA_log").n)
        q.wait_ge(S("sA_epi").h, S("sA_epi").n)
        q.wait_ge(S("sA_epp").h, S("sA_epp").n)
        q.wait_ge(S("sA_xnt").h, S("sA_xnt").n)
        q.wait_ge(S("sA_aux").h, S("sA_aux").n)
    block.scalar(bar)

    def bar2(v):
        v.wait_ge(S("sA_epp").h, S("sA_epp").n)
        v.wait_ge(S("sA_rope").h, S("sA_rope").n)
    block.vector(bar2)

    def bar3(t):
        t.wait_ge(S("sA_epi").h, S("sA_epi").n)
        t.wait_ge(S("sA_rope").h, S("sA_rope").n)
        t.wait_ge(S("sA_xnt").h, S("sA_xnt").n)
    block.tensor(bar3)

    def bar4(p):
        p.wait_ge(S("sA_epi").h, S("sA_epi").n)
    block.gpsimd(bar4)


def emit_dbgA(g):
    S, block = g.S, g.block

    def dump(q):
        q.wait_ge(S("sA_rope").h, S("sA_rope").n)
        q.wait_ge(S("sA_vev").h, S("sA_vev").n)
        q.wait_ge(S("sA_log").h, S("sA_log").n)
        q.wait_ge(S("sA_epi").h, S("sA_epi").n)
        q.wait_ge(S("sA_epp").h, S("sA_epp").n)
        q.wait_ge(S("sA_xnt").h, S("sA_xnt").n)
        s = S("s_dbg")
        if "qkv" in g.feat:
            s.dinc(q.dma_start(out=g.dbg_kt[:, :], in_=g.kt[:, :]))
            s.dinc(q.dma_start(out=g.dbg_v[:, :], in_=g.vsb[:, :]))
        if "own" in g.feat:
            s.dinc(q.dma_start(out=g.dbg_log[:, :], in_=g.logits[:, :]))
        if "epi" in g.feat:
            s.dinc(q.dma_start(out=g.dbg_aux[:, 0:NT], in_=g.kmaxb[:, :]))
            s.dinc(q.dma_start(out=g.dbg_aux[:, NT:3 * NT], in_=g.qn[:, :]))
            s.dinc(q.dma_start(out=g.dbg_aux[:, 3 * NT:3 * NT + 2], in_=g.scol[:, :]))
        q.wait_ge(s.h, s.n)
    block.scalar(dump)


def emit_stub_out(g):
    S, block = g.S, g.block

    def stub(q):
        for sname in list(g.sems):
            pass
        s = S("s_out")
        s.dinc(q.dma_start(out=g.out[0:128, :], in_=g.x_own[0:128, :]))
        q.wait_ge(s.h, s.n)
    block.gpsimd(stub)




def emit_setup(g):
    S, block = g.S, g.block

    def dmas(q):
        s = S("s_const")
        for dst, src in [(g.c_idbf, g.ident_bf), (g.c_idf, g.ident_f),
                         (g.c_onesf, g.ones_f), (g.c_onesbf, g.ones_bf),
                         (g.c_mask, g.mask_f), (g.c_dirc, g.dirs_c),
                         (g.c_dirb, g.dirs_b), (g.ridx, g.rowidx),
                         (g.hs_sb, g.hs), (g.actc_sb, g.actc)]:
            s.dinc(q.dma_start(out=dst[:, :], in_=src[:, :]))
    block.sync(dmas)


def emit_phaseA(g):
    nc, S, block, sb, ps = g.nc, g.S, g.block, g.sb, g.ps
    g.es_a = ExitStack()
    esa = g.es_a

    g.xtile = [sb(f"xtile{j}", [128, C], F32, esa) for j in range(2)]
    g.sq_scr = sb("sq_scr", [128, C], F32, esa)
    g.xn_tm = [sb(f"xn_tm{j}", [128, C], BF16, esa) for j in range(2)]
    g.xnT = sb("xnT", [128, NCC * 512], BF16, esa)
    g.cs_cos = sb("cs_cos", [128, 512], F32, esa)
    g.cs_sin = sb("cs_sin", [128, 512], F32, esa)
    g.wq_sb = sb("wq_sb", [128, NCC * 512], BF16, esa)
    g.wk_sb = sb("wk_sb", [128, NCC * 256], BF16, esa)
    g.wv_sb = sb("wv_sb", [128, NCC * 256], BF16, esa)
    g.wstage = [sb(f"wstage{j}", [128, 512], F32, esa) for j in range(2)]
    g.qstage = [sb(f"qstage{j}", [128, 512], BF16, esa) for j in range(4)]
    g.rp_t = [sb(f"rp_t{j}", [128, 512], F32, esa) for j in range(2)]
    g.sqq = [sb(f"sqq{j}", [128, 512], BF16, esa) for j in range(2)]
    g.mini = sb("mini", [128, 32], F32, esa)
    g.xnf = sb("xnf", [128, C], F32, esa)
    g.xnT32 = sb("xnT32", [128, NCC * 128], F32, esa)
    g.gwsg_sb = sb("gwsg_sb", [128, NCC * 16], F32, esa)

    g.ptA = ps("ptA", [128, 512], BF16, esa)
    g.ptB = ps("ptB", [128, 512], BF16, esa)
    g.pq = [ps("pq0", [128, 512], F32, esa), ps("pq1", [128, 512], F32, esa)]
    g.pk = [ps("pk0", [128, 512], F32, esa), ps("pk1", [128, 512], F32, esa)]
    g.pv = ps("pv", [128, 512], F32, esa)
    g.paux = ps("paux", [128, 512], F32, esa)

    sld, scast = S("sA_wld"), S("sA_wcast")

    def wload(q):
        jobs = ([("q", cc) for cc in range(NCC)] + [("k", cc) for cc in range(NCC)]
                + [("v", cc) for cc in range(NCC)])
        for j, (kind, cc) in enumerate(jobs):
            if j >= 2:
                q.wait_ge(scast.h, j - 1)
            if kind == "q":
                sld.dinc(q.dma_start(out=g.wstage[j % 2][:, 0:512],
                                     in_=g.wq[cc * 128:(cc + 1) * 128, :]))
            elif kind == "k":
                sld.dinc(q.dma_start(out=g.wstage[j % 2][:, 0:256],
                                     in_=g.wk[cc * 128:(cc + 1) * 128, :]))
            else:
                sld.dinc(q.dma_start(out=g.wstage[j % 2][:, 0:256],
                                     in_=g.wv[cc * 128:(cc + 1) * 128, :]))
        for cc in range(NCC):
            sld.dinc(q.dma_start(out=g.gwsg_sb[:, cc * 16:(cc + 1) * 16],
                                 in_=g.gwsg_t[cc * 128:(cc + 1) * 128, :]))
    block.sync(wload)

    def wcast(p):
        jobs = ([(g.wq_sb, cc, 512) for cc in range(NCC)]
                + [(g.wk_sb, cc, 256) for cc in range(NCC)]
                + [(g.wv_sb, cc, 256) for cc in range(NCC)])
        for j, (dst, cc, w) in enumerate(jobs):
            p.wait_ge(sld.h, (j + 1) * 16)
            scast.inc(p.tensor_copy(out=dst[:, cc * w:(cc + 1) * w],
                                    in_=g.wstage[j % 2][:, 0:w]))
    block.gpsimd(wcast)

    if "ln" in g.feat:
        for ch in range(NCC):
            _chunkA(g, ch, own=False)
    if "own" in g.feat:
        for ch in range(4):
            _chunkA(g, ch, own=True)


def _chunkA(g, ch, own):
    nc, S, block = g.nc, g.S, g.block
    xsrc = g.x_own if own else g.x
    sX, sCS = S("sA_x"), S("sA_cs")
    sSXX, sVAR, sRSTD, sXN = S("sA_sxx"), S("sA_var"), S("sA_rstd"), S("sA_xn")
    sTP, sXNT = S("sA_tp"), S("sA_xnt")
    sQMM, sROPE, sSQQ, sPQN = S("sA_qmm"), S("sA_rope"), S("sA_sqq"), S("sA_pqn")
    sAUX, sQW = S("sA_aux"), S("sA_qw")
    sVMM, sVEV = S("sA_vmm"), S("sA_vev")
    sT32, sT32E, sRT, sLOG = S("sA_t32"), S("sA_t32e"), S("sA_rt"), S("sA_log")
    sE, sEA = S("sA_e"), S("sA_ea")

    mini, xtile, xn_tm, xnT = g.mini, g.xtile, g.xn_tm, g.xnT

    xn_base = sXN.n

    def dmax(q, ch=ch, xn_base=xn_base):
        for ts in range(4):
            tno = xn_base + ts
            if tno >= 2:
                q.wait_ge(sXN.h, tno - 1)
            sX.dinc(q.dma_start(out=xtile[ts % 2][:, :],
                                in_=xsrc[ch * 512 + ts * 128: ch * 512 + ts * 128 + 128, :]))
        if not own:
            sROPE.wait(q)
            sCS.dinc(q.dma_start(out=g.cs_cos[:, :], in_=g.cos_t[:, ch * 512:(ch + 1) * 512]))
            sCS.dinc(q.dma_start(out=g.cs_sin[:, :], in_=g.sin_t[:, ch * 512:(ch + 1) * 512]))
    block.sync(dmax)

    for ts in range(4):
        mo = ts * 8
        xw = sX.n - 16 * (3 - ts)

        def ln_d1(v, ts=ts, mo=mo, xw=xw):
            v.wait_ge(sX.h, xw)
            sE.inc(v.reduce_sum(mini[:, mo:mo + 1], xtile[ts % 2][:, :], axis=AX.X))
            v.wait_ge(sE.h, sE.n)
            sE.inc(v.tensor_scalar(out=mini[:, mo + 1:mo + 2], in0=mini[:, mo:mo + 1],
                                   scalar1=1.0 / C, scalar2=None, op0=ALU.mult))
        block.vector(ln_d1)

        def ln_a1(s, ts=ts, mo=mo, xw=xw):
            s.wait_ge(sX.h, xw)
            sSXX.inc(s.activation(out=g.sq_scr[:, :], in_=xtile[ts % 2][:, :],
                                  func=AF.Square, accum_out=mini[:, mo + 3:mo + 4]))
        block.scalar(ln_a1)

        def ln_d2(v, ts=ts, mo=mo):
            v.wait_ge(sSXX.h, sSXX.n)
            v.wait_ge(sE.h, sE.n)
            v.tensor_scalar(out=mini[:, mo + 2:mo + 3], in0=mini[:, mo + 1:mo + 2],
                            scalar1=mini[:, mo + 1:mo + 2], scalar2=None, op0=ALU.mult)
            sE.inc(v.tensor_scalar(out=mini[:, mo + 4:mo + 5], in0=mini[:, mo + 3:mo + 4],
                                   scalar1=1.0 / C, scalar2=None, op0=ALU.mult))
            v.wait_ge(sE.h, sE.n)
            sVAR.inc(v.tensor_sub(mini[:, mo + 5:mo + 6], mini[:, mo + 4:mo + 5],
                                  mini[:, mo + 2:mo + 3]))
        block.vector(ln_d2)

        def ln_a2(s, ts=ts, mo=mo):
            s.wait_ge(sVAR.h, sVAR.n)
            sEA.inc(s.activation(out=mini[:, mo + 6:mo + 7], in_=mini[:, mo + 5:mo + 6],
                                 func=AF.Ln, bias=g.actc_sb[:, 0:1], scale=1.0))
            s.wait_ge(sEA.h, sEA.n)
            sRSTD.inc(s.activation(out=mini[:, mo + 7:mo + 8], in_=mini[:, mo + 6:mo + 7],
                                   func=AF.Exp, bias=0.0, scale=-0.5))
        block.scalar(ln_a2)

        def ln_d3(v, ts=ts, mo=mo, own=own):
            v.wait_ge(sRSTD.h, sRSTD.n)
            ins = v.tensor_scalar(out=xn_tm[ts % 2][:, :], in0=xtile[ts % 2][:, :],
                                  scalar1=mini[:, mo + 1:mo + 2],
                                  scalar2=mini[:, mo + 7:mo + 8],
                                  op0=ALU.subtract, op1=ALU.mult)
            if own:
                ins = v.tensor_scalar(out=g.xnf[:, :], in0=xtile[ts % 2][:, :],
                                      scalar1=mini[:, mo + 1:mo + 2],
                                      scalar2=mini[:, mo + 7:mo + 8],
                                      op0=ALU.subtract, op1=ALU.mult)
            sXN.inc(ins)
        block.vector(ln_d3)

        if "tp" not in g.feat:
            continue
        baseTG = sTP.n
        baseEG = sXNT.n

        def tp_pe(t, ts=ts, baseTG=baseTG):
            t.wait_ge(sXN.h, sXN.n)
            for gi in range(4):
                G = baseTG + gi
                if G >= 2:
                    t.wait_ge(sXNT.h, G - 1)
                pt = g.ptA if G % 2 == 0 else g.ptB
                for j in range(4):
                    cc = gi * 4 + j
                    ins = t.transpose(pt[:, j * 128:(j + 1) * 128],
                                      xn_tm[ts % 2][:, cc * 128:(cc + 1) * 128],
                                      g.c_idbf[:, :])
                sTP.inc(ins)
        block.tensor(tp_pe)

        first_tile_of_chunk = (ts == 0)
        vmm_gate = sVMM.n
        qw_gate = sQW.n
        xnT3 = xnT[:, :].rearrange("p (c j) -> p c j", j=512)

        def tp_ev(v, ts=ts, baseTG=baseTG, own=own, first=first_tile_of_chunk,
                  vmm_gate=vmm_gate, qw_gate=qw_gate, xnT3=xnT3):
            if first:
                if own:
                    if qw_gate > 0:
                        v.wait_ge(sQW.h, qw_gate)
                elif vmm_gate > 0:
                    v.wait_ge(sVMM.h, vmm_gate)
            for gi in range(4):
                G = baseTG + gi
                v.wait_ge(sTP.h, G + 1)
                pt = g.ptA if G % 2 == 0 else g.ptB
                sXNT.inc(v.tensor_copy(
                    out=xnT3[:, gi * 4:(gi + 1) * 4, ts * 128:(ts + 1) * 128],
                    in_=pt[:, :].rearrange("p (c j) -> p c j", j=128)))
        block.vector(tp_ev)

        if own:
            base32 = sT32.n
            rope_gate = sROPE.n

            def tp32_pe(t, ts=ts, base32=base32, rope_gate=rope_gate):
                t.wait_ge(sXN.h, sXN.n)
                if rope_gate > 0:
                    t.wait_ge(sROPE.h, rope_gate)
                for gi in range(4):
                    G = base32 + gi
                    if G >= 2:
                        t.wait_ge(sT32E.h, G - 1)
                    pt = g.pq[G % 2]
                    for j in range(4):
                        cc = gi * 4 + j
                        ins = t.transpose(pt[:, j * 128:(j + 1) * 128],
                                          g.xnf[:, cc * 128:(cc + 1) * 128],
                                          g.c_idf[:, :])
                    sT32.inc(ins)
            block.tensor(tp32_pe)

            rt_gate = sRT.n

            def tp32_ev(v, ts=ts, base32=base32, rt_gate=rt_gate):
                if rt_gate > 0:
                    v.wait_ge(sRT.h, rt_gate)
                for gi in range(4):
                    G = base32 + gi
                    v.wait_ge(sT32.h, G + 1)
                    pt = g.pq[G % 2]
                    sT32E.inc(v.tensor_copy(
                        out=g.xnT32[:, gi * 512:(gi + 1) * 512], in_=pt[:, :]))
            block.vector(tp32_ev)

            def rt_pe(t, ts=ts):
                t.wait_ge(sT32E.h, sT32E.n)
                t.wait_ge(S("sA_wld").h, S("sA_wld").n)
                if sLOG.n > 0:
                    t.wait_ge(sLOG.h, sLOG.n)
                for cc in range(NCC):
                    ins = t.matmul(g.paux[:, 0:16], g.xnT32[:, cc * 128:(cc + 1) * 128],
                                   g.gwsg_sb[:, cc * 16:(cc + 1) * 16],
                                   start=(cc == 0), stop=(cc == NCC - 1))
                    if cc == NCC - 1:
                        sRT.inc(ins)
            block.tensor(rt_pe)

            idx = ch * 4 + ts

            def rt_ev(v, idx=idx):
                v.wait_ge(sRT.h, sRT.n)
                sLOG.inc(v.tensor_copy(out=g.logits[:, idx * 16:(idx + 1) * 16],
                                       in_=g.paux[:, 0:16]))
            block.vector(rt_ev)

    if own:
        def xnw(q, ch=ch):
            q.wait_ge(sXNT.h, sXNT.n)
            for cc in range(NCC):
                sQW.dinc(q.dma_start(out=g.xn_scr.ap()[cc, :, ch * 512:(ch + 1) * 512],
                                     in_=xnT[:, cc * 512:(cc + 1) * 512]))
        block.scalar(xnw)
        return

    if "qkv" not in g.feat:
        return
    for h in range(2):
        def q_pe(t, h=h, ch=ch):
            t.wait_ge(sXNT.h, sXNT.n)
            t.wait_ge(S("sA_wcast").h, S("sA_wcast").n)
            if sROPE.n > 0:
                t.wait_ge(sROPE.h, sROPE.n)
            for dhc in range(2):
                for cc in range(NCC):
                    o = cc * 512 + h * 256 + dhc * 128
                    ins = t.matmul(g.pq[dhc][:, :], g.wq_sb[:, o:o + 128],
                                   xnT[:, cc * 512:(cc + 1) * 512],
                                   start=(cc == 0), stop=(cc == NCC - 1))
                    if cc == NCC - 1:
                        sQMM.inc(ins)
        block.tensor(q_pe)

        def q_rope(v, h=h, ch=ch):
            v.wait_ge(sQMM.h, sQMM.n)
            v.wait_ge(sCS.h, sCS.n)
            if sQW.n > 0:
                v.wait_ge(sQW.h, sQW.n)
            if sSQQ.n > 0:
                v.wait_ge(sSQQ.h, sSQQ.n)
            v.tensor_mul(g.rp_t[0][:, :], g.pq[0][:, :], g.cs_cos[:, :])
            v.tensor_mul(g.rp_t[1][:, :], g.pq[1][:, :], g.cs_sin[:, :])
            v.tensor_sub(g.qstage[h * 2][:, :], g.rp_t[0][:, :], g.rp_t[1][:, :])
            v.tensor_mul(g.rp_t[0][:, :], g.pq[1][:, :], g.cs_cos[:, :])
            v.tensor_mul(g.rp_t[1][:, :], g.pq[0][:, :], g.cs_sin[:, :])
            sROPE.inc(v.tensor_add(g.qstage[h * 2 + 1][:, :], g.rp_t[0][:, :],
                                   g.rp_t[1][:, :]))
        block.vector(q_rope)

        def q_sq(s, h=h):
            s.wait_ge(sROPE.h, sROPE.n)
            if sPQN.n > 0:
                s.wait_ge(sPQN.h, sPQN.n)
            sEA.inc(s.activation(out=g.sqq[0][:, :], in_=g.qstage[h * 2][:, :],
                                 func=AF.Square))
            sSQQ.inc(s.activation(out=g.sqq[1][:, :], in_=g.qstage[h * 2 + 1][:, :],
                                  func=AF.Square))
        if "aux" in g.feat:
            block.scalar(q_sq)

        def qn_pe(t, h=h, ch=ch):
            t.wait_ge(sSQQ.h, sSQQ.n)
            t.wait_ge(sAUX.h, sAUX.n)
            for tsub in range(4):
                for dhc in range(2):
                    ins = t.matmul(g.pv[:, tsub:tsub + 1],
                                   g.sqq[dhc][:, tsub * 128:(tsub + 1) * 128],
                                   g.c_onesbf[:, :], start=(dhc == 0), stop=(dhc == 1))
                    if dhc == 1 and tsub == 3:
                        sPQN.inc(ins)
        if "aux" in g.feat:
            block.tensor(qn_pe)

        def qn_ev(v, h=h, ch=ch):
            v.wait_ge(sPQN.h, sPQN.n)
            sAUX.inc(v.tensor_copy(out=g.qn2[:, h * NT + ch * 4: h * NT + ch * 4 + 4],
                                   in_=g.pv[:, 0:4]))
        if "aux" in g.feat:
            block.vector(qn_ev)

        def q_out(q, h=h, ch=ch):
            q.wait_ge(sROPE.h, sROPE.n)
            for dhc in range(2):
                sQW.dinc(q.dma_start(out=g.q_scr.ap()[h, dhc, :, ch * 512:(ch + 1) * 512],
                                     in_=g.qstage[h * 2 + dhc][:, :]))
        block.scalar(q_out)

    def k_pe(t, ch=ch):
        t.wait_ge(sXNT.h, sXNT.n)
        t.wait_ge(S("sA_wcast").h, S("sA_wcast").n)
        if sROPE.n > 0:
            t.wait_ge(sROPE.h, sROPE.n)
        for dhc in range(2):
            for cc in range(NCC):
                o = cc * 256 + dhc * 128
                ins = t.matmul(g.pk[dhc][:, :], g.wk_sb[:, o:o + 128],
                               xnT[:, cc * 512:(cc + 1) * 512],
                               start=(cc == 0), stop=(cc == NCC - 1))
                if cc == NCC - 1:
                    sQMM.inc(ins)
    block.tensor(k_pe)

    def k_rope(v, ch=ch):
        v.wait_ge(sQMM.h, sQMM.n)
        v.wait_ge(sCS.h, sCS.n)
        o0, o1 = ch * 512, T + ch * 512
        v.tensor_mul(g.rp_t[0][:, :], g.pk[0][:, :], g.cs_cos[:, :])
        v.tensor_mul(g.rp_t[1][:, :], g.pk[1][:, :], g.cs_sin[:, :])
        v.tensor_sub(g.kt[:, o0:o0 + 512], g.rp_t[0][:, :], g.rp_t[1][:, :])
        v.tensor_mul(g.rp_t[0][:, :], g.pk[1][:, :], g.cs_cos[:, :])
        v.tensor_mul(g.rp_t[1][:, :], g.pk[0][:, :], g.cs_sin[:, :])
        sROPE.inc(v.tensor_add(g.kt[:, o1:o1 + 512], g.rp_t[0][:, :], g.rp_t[1][:, :]))
    block.vector(k_rope)

    def k_sq(s, ch=ch):
        s.wait_ge(sROPE.h, sROPE.n)
        s.wait_ge(sPQN.h, sPQN.n)
        o0, o1 = ch * 512, T + ch * 512
        sEA.inc(s.activation(out=g.sqq[0][:, :], in_=g.kt[:, o0:o0 + 512], func=AF.Square))
        sSQQ.inc(s.activation(out=g.sqq[1][:, :], in_=g.kt[:, o1:o1 + 512], func=AF.Square))
    if "aux" in g.feat:
        block.scalar(k_sq)

    def kn_pe(t, ch=ch):
        t.wait_ge(sSQQ.h, sSQQ.n)
        t.wait_ge(sAUX.h, sAUX.n)
        for dhc in range(2):
            ins = t.matmul(g.paux[0:1, 0:512], g.c_onesbf[:, :], g.sqq[dhc][:, :],
                           start=(dhc == 0), stop=(dhc == 1))
            if dhc == 1:
                sPQN.inc(ins)
    if "aux" in g.feat:
        block.tensor(kn_pe)

    def kn_ev(v, ch=ch):
        v.wait_ge(sPQN.h, sPQN.n)
        for sub in range(4):
            sAUX.inc(v.reduce_max(g.kn2t[0:1, ch * 4 + sub:ch * 4 + sub + 1],
                                  g.paux[0:1, sub * 128:(sub + 1) * 128], axis=AX.X))
    if "aux" in g.feat:
        block.vector(kn_ev)

    for tsub in range(4):
        def v_pe(t, ch=ch, tsub=tsub):
            t.wait_ge(sXNT.h, sXNT.n)
            t.wait_ge(S("sA_wcast").h, S("sA_wcast").n)
            if sVEV.n > 0:
                t.wait_ge(sVEV.h, sVEV.n)
            if tsub == 0:
                t.wait_ge(sAUX.h, sAUX.n)
            for cc in range(NCC):
                ins = t.matmul(g.pv[:, 0:256],
                               xnT[:, cc * 512 + tsub * 128: cc * 512 + tsub * 128 + 128],
                               g.wv_sb[:, cc * 256:(cc + 1) * 256],
                               start=(cc == 0), stop=(cc == NCC - 1))
                if cc == NCC - 1:
                    sVMM.inc(ins)
        block.tensor(v_pe)

        def v_ev(v, ch=ch, tsub=tsub):
            v.wait_ge(sVMM.h, sVMM.n)
            kc = ch * 4 + tsub
            sVEV.inc(v.tensor_copy(out=g.vsb[:, kc * 256:(kc + 1) * 256],
                                   in_=g.pv[:, 0:256]))
        block.vector(v_ev)


def emit_phaseA_epi(g):
    nc, S, block = g.nc, g.S, g.block
    sAUX, sEPI, sE, sEA, sEPP = S("sA_aux"), S("sA_epi"), S("sA_e"), S("sA_ea"), S("sA_epp")

    def epi_dve(v):
        v.wait_ge(sAUX.h, sAUX.n)
        src, dst = g.kn2t, g.tabA
        k = 1
        while k < NT:
            v.tensor_copy(out=dst[0:1, 0:k], in_=src[0:1, 0:k])
            sE.inc(v.tensor_max(dst[0:1, k:NT], src[0:1, k:NT], src[0:1, 0:NT - k]))
            v.wait_ge(sE.h, sE.n)
            src, dst = dst, (g.kn2t if dst is g.tabA else g.tabA)
            k *= 2
        sEPI.inc(v.tensor_copy(out=g.tabB[0:1, 0:NT], in_=src[0:1, 0:NT]))
    block.vector(epi_dve)

    def epi_act(s):
        s.wait_ge(sEPI.h, sEPI.n)
        sEA.inc(s.activation(out=g.tabA[0:1, 0:NT], in_=g.tabB[0:1, 0:NT],
                             func=AF.Ln, bias=g.actc_sb[0:1, 1:2], scale=1.0))
        s.wait_ge(sEA.h, sEA.n)
        sEA.inc(s.activation(out=g.tabB[0:1, 0:NT], in_=g.tabA[0:1, 0:NT],
                             func=AF.Exp, bias=0.0, scale=0.5))
        s.wait_ge(sEA.h, sEA.n)
        sEA.inc(s.activation(out=g.qn[:, :], in_=g.qn2[:, :],
                             func=AF.Ln, bias=g.actc_sb[:, 1:2], scale=1.0))
        s.wait_ge(sEA.h, sEA.n)
        sEPP.inc(s.activation(out=g.qn[:, :], in_=g.qn[:, :],
                              func=AF.Exp, bias=0.0, scale=0.5))
    block.scalar(epi_act)

    def epi_pe(t):
        t.wait_ge(sEPP.h, sEPP.n)
        t.wait_ge(S("s_const").h, S("s_const").n)
        t.wait_ge(S("sA_log").h, S("sA_log").n)
        sEPP.inc(t.matmul(g.paux[:, 0:NT], g.c_onesf[0:1, :], g.tabB[0:1, 0:NT],
                          start=True, stop=True))
        sEPP.inc(t.matmul(g.paux[:, 256:258], g.c_onesf[0:1, :], g.hs_sb[0:1, 0:2],
                          start=True, stop=True))
    block.tensor(epi_pe)

    def epi_ev(v):
        v.wait_ge(sEPP.h, sEPP.n)
        v.tensor_copy(out=g.kmaxb[:, :], in_=g.paux[:, 0:NT])
        sE.inc(v.tensor_copy(out=g.scol[:, :], in_=g.paux[:, 256:258]))
        v.wait_ge(sE.h, sE.n)
        sEPI.inc(v.tensor_scalar(out=g.omscol[:, :], in0=g.scol[:, :],
                                 scalar1=-1.0, scalar2=1.0, op0=ALU.mult, op1=ALU.add))
    block.vector(epi_ev)


def emit_phaseB(g):
    nc, S, block, sb, ps = g.nc, g.S, g.block, g.sb, g.ps
    g.es_a.close()
    g.es_b = ExitStack()
    esb = g.es_b

    g.kp_b = sb("kp_b", [128, T], F32, esb)
    g.expP = sb("expP", [128, T], BF16, esb)
    g.expG = sb("expG", [128, T], BF16, esb)
    g.gbuf = [sb(f"gbuf{j}", [128, 512], F32, esb) for j in range(2)]
    g.mt = sb("mt", [128, 8 * 128], BF16, esb)
    g.qtile = [sb(f"qtile{j}", [128, 256], BF16, esb) for j in range(2)]
    g.astage = [sb(f"astage{j}", [128, 256], BF16, esb) for j in range(2)]
    g.accP = sb("accP", [128, 16], F32, esb)
    g.accG = sb("accG", [128, 16], F32, esb)
    g.bmini = sb("bmini", [128, 16], F32, esb)

    g.ps_s = [ps(f"ps_s{j}", [128, 512], F32, esb) for j in range(3)]
    g.pt_b = [ps(f"pt_b{j}", [128, 512], BF16, esb) for j in range(2)]
    g.pa = [ps(f"pa{j}", [128, 512], F32, esb) for j in range(2)]
    g.pb0 = ps("pb0", [128, 512], F32, esb)

    sQT, sQP, sQPE, sMini = S("sB_qt"), S("sB_qp"), S("sB_qpe"), S("sB_mini")
    sSC, sMask, sEXP = S("sB_sc"), S("sB_mask"), S("sB_exp")
    sGeo, sEXG = S("sB_geo"), S("sB_exg")
    sBl1, sBl2 = S("sB_bl1"), S("sB_bl2")
    sTPb, sMT, sAMM, sAEV, sAO = S("sB_tp"), S("sB_mt"), S("sB_amm"), S("sB_aev"), S("sB_ao")
    sKP, sKPEV = S("sB_kp"), S("sB_kpev")
    sE = S("sB_e")

    unit_qfree = []
    sbank_free = [0, 0, 0]
    gbuf_free = [0, 0]

    for h in range(2):
        base_kpev = sKPEV.n
        rope_tot = S("sA_rope").n
        aux_tot = S("sA_aux").n
        epi_tot = S("sA_epi").n

        xnt_tot = S("sA_xnt").n
        t32e_tot = S("sA_t32e").n
        log_tot = S("sA_log").n
        qw_tot = S("sA_qw").n

        def pre_pe(t, h=h, base_kpev=base_kpev):
            if h == 0:
                t.wait_ge(S("sA_epi").h, epi_tot)
                t.wait_ge(S("sA_rope").h, rope_tot)
                t.wait_ge(S("sA_aux").h, aux_tot)
                t.wait_ge(S("sA_xnt").h, xnt_tot)
                if t32e_tot > 0:
                    t.wait_ge(S("sA_t32e").h, t32e_tot)
                if log_tot > 0:
                    t.wait_ge(S("sA_log").h, log_tot)
                t.wait_ge(S("sA_qw").h, qw_tot)
            else:
                t.wait_ge(S("sB_geo").h, S("sB_geo").n)
            for cw in range(NCC):
                if cw >= 2:
                    t.wait_ge(sKPEV.h, base_kpev + cw - 1)
                pt = g.pb0 if cw % 2 == 0 else g.ps_s[2]
                sKP.inc(t.matmul(pt[:, :], g.c_dirb[0:3, h * 128:(h + 1) * 128],
                                 g.kt[0:3, cw * 512:(cw + 1) * 512], start=True, stop=True))
        block.tensor(pre_pe)

        base_kp = sKP.n - NCC

        def pre_ev(v, h=h, base_kp=base_kp):
            for cw in range(NCC):
                v.wait_ge(sKP.h, base_kp + cw + 1)
                pt = g.pb0 if cw % 2 == 0 else g.ps_s[2]
                sKPEV.inc(v.tensor_copy(out=g.kp_b[:, cw * 512:(cw + 1) * 512], in_=pt[:, :]))
            for i in range(NT):
                v.tensor_reduce(g.tabA[0:1, i:i + 1], g.kp_b[0:1, i * 128:(i + 1) * 128],
                                axis=AX.X, op=ALU.max)
                sE.inc(v.tensor_reduce(g.kn2t[0:1, i:i + 1],
                                       g.kp_b[0:1, i * 128:(i + 1) * 128],
                                       axis=AX.X, op=ALU.min))
            v.wait_ge(sE.h, sE.n)
            src, dst = g.tabA, g.tabB
            k = 1
            while k < NT:
                v.tensor_copy(out=dst[0:1, 0:k], in_=src[0:1, 0:k])
                sE.inc(v.tensor_max(dst[0:1, k:NT], src[0:1, k:NT], src[0:1, 0:NT - k]))
                v.wait_ge(sE.h, sE.n)
                src, dst = dst, src
                k *= 2
            if src is not g.tabA:
                sE.inc(v.tensor_copy(out=g.tabA[0:1, 0:NT], in_=src[0:1, 0:NT]))
                v.wait_ge(sE.h, sE.n)
            src2, dst2 = g.kn2t, g.tabB
            k = 1
            while k < NT:
                v.tensor_copy(out=dst2[0:1, 0:k], in_=src2[0:1, 0:k])
                sE.inc(v.tensor_tensor(out=dst2[0:1, k:NT], in0=src2[0:1, k:NT],
                                       in1=src2[0:1, 0:NT - k], op=ALU.min))
                v.wait_ge(sE.h, sE.n)
                src2, dst2 = dst2, src2
                k *= 2
            if src2 is not g.kn2t:
                sE.inc(v.tensor_copy(out=g.kn2t[0:1, 0:NT], in_=src2[0:1, 0:NT]))
                v.wait_ge(sE.h, sE.n)
            sKPEV.inc(v.engine_nop())
        block.vector(pre_ev)

        def pre_pe2(t, h=h):
            t.wait_ge(sKPEV.h, sKPEV.n)
            sKP.inc(t.matmul(g.pb0[:, 0:NT], g.c_onesf[0:1, :], g.tabA[0:1, 0:NT],
                             start=True, stop=True))
            sKP.inc(t.matmul(g.pb0[:, 64:64 + NT], g.c_onesf[0:1, :], g.kn2t[0:1, 0:NT],
                             start=True, stop=True))
        block.tensor(pre_pe2)

        def pre_ev2(v, h=h):
            v.wait_ge(sKP.h, sKP.n)
            v.tensor_copy(out=g.cmaxb[:, :], in_=g.pb0[:, 0:NT])
            sKPEV.inc(v.tensor_copy(out=g.cminb[:, :], in_=g.pb0[:, 64:64 + NT]))
        block.vector(pre_ev2)

        for i in range(NT):
            Ki = (i + 1) * 128
            ncki = (Ki + 511) // 512
            lastw = Ki - (ncki - 1) * 512
            u = h * NT + i

            def qdma(q, h=h, i=i, u=u):
                if u >= 2:
                    q.wait_ge(sSC.h, unit_qfree[u - 2])
                if u == 0:
                    q.wait_ge(S("sA_qw").h, S("sA_qw").n)
                    q.wait_ge(S("sA_epp").h, S("sA_epp").n)
                    q.wait_ge(S("sA_log").h, S("sA_log").n)
                for dhc in range(2):
                    sQT.dinc(q.dma_start(out=g.qtile[u % 2][:, dhc * 128:(dhc + 1) * 128],
                                         in_=g.q_scr.ap()[h, dhc, :, i * 128:(i + 1) * 128]))
            block.sync(qdma)

            def qp_pe(t, h=h, i=i, u=u):
                t.wait_ge(sQT.h, sQT.n)
                if sQPE.n > 0:
                    t.wait_ge(sQPE.h, sQPE.n)
                t.wait_ge(sKPEV.h, sKPEV.n)
                sQP.inc(t.matmul(g.pb0[:, 0:1], g.qtile[u % 2][0:3, 0:128],
                                 g.c_dirc[0:3, h:h + 1], start=True, stop=True))
            block.tensor(qp_pe)

            def minis(v, h=h, i=i, u=u):
                v.wait_ge(sQP.h, sQP.n)
                if sEXG.n > 0:
                    v.wait_ge(sEXG.h, sEXG.n)
                if sBl2.n > 0:
                    v.wait_ge(sBl2.h, sBl2.n)
                sQPE.inc(v.tensor_copy(out=g.bmini[:, 0:1], in_=g.pb0[:, 0:1]))
                v.wait_ge(sQPE.h, sQPE.n)
                v.tensor_scalar(out=g.bmini[:, 1:2], in0=g.kmaxb[:, i:i + 1],
                                scalar1=g.qn[:, h * NT + i:h * NT + i + 1],
                                scalar2=-SCALE, op0=ALU.mult, op1=ALU.mult)
                sE.inc(v.tensor_scalar(out=g.bmini[:, 2:3], in0=g.bmini[:, 0:1],
                                       scalar1=-1.0, scalar2=None, op0=ALU.mult))
                v.wait_ge(sE.h, sE.n)
                v.tensor_scalar(out=g.bmini[:, 3:4], in0=g.cmaxb[:, i:i + 1],
                                scalar1=g.bmini[:, 2:3], scalar2=None, op0=ALU.mult)
                sE.inc(v.tensor_scalar(out=g.bmini[:, 4:5], in0=g.cminb[:, i:i + 1],
                                       scalar1=g.bmini[:, 2:3], scalar2=None, op0=ALU.mult))
                v.wait_ge(sE.h, sE.n)
                sMini.inc(v.tensor_tensor(out=g.bmini[:, 5:6], in0=g.bmini[:, 3:4],
                                          in1=g.bmini[:, 4:5], op=ALU.min))
            block.vector(minis)

            for kc in range(ncki):
                w = 512 if kc < ncki - 1 else lastw
                bank = g.ps_s[kc % 3]
                isdiag = (kc == ncki - 1)

                bfree = sbank_free[kc % 3]

                def sc_pe(t, h=h, i=i, u=u, kc=kc, w=w, bank=bank, bfree=bfree):
                    t.wait_ge(sQT.h, sQT.n)
                    if u == 0 and kc < 3:
                        t.wait_ge(sKPEV.h, sKPEV.n)
                    if bfree > 0:
                        t.wait_ge(sEXP.h, bfree)
                    for dhc in range(2):
                        ins = t.matmul(bank[:, 0:w],
                                       g.qtile[u % 2][:, dhc * 128:(dhc + 1) * 128],
                                       g.kt[:, dhc * T + kc * 512: dhc * T + kc * 512 + w],
                                       start=(dhc == 0), stop=(dhc == 1))
                        if dhc == 1:
                            sSC.inc(ins)
                block.tensor(sc_pe)

                if isdiag:
                    def msk(v, kc=kc, w=w, bank=bank):
                        v.wait_ge(sSC.h, sSC.n)
                        sMask.inc(v.tensor_add(bank[:, w - 128:w], bank[:, w - 128:w],
                                               g.c_mask[:, :]))
                    block.vector(msk)

                def expp(s, kc=kc, w=w, bank=bank, isdiag=isdiag):
                    if isdiag:
                        s.wait_ge(sMask.h, sMask.n)
                    else:
                        s.wait_ge(sSC.h, sSC.n)
                    if kc == 0:
                        s.wait_ge(sMini.h, sMini.n)
                        if sBl2.n > 0:
                            s.wait_ge(sBl2.h, sBl2.n)
                    sEXP.inc(s.activation(out=g.expP[:, kc * 512:kc * 512 + w],
                                          in_=bank[:, 0:w], func=AF.Exp,
                                          bias=g.bmini[:, 1:2], scale=SCALE,
                                          accum_out=g.accP[:, kc:kc + 1]))
                block.scalar(expp)
                sbank_free[kc % 3] = sEXP.n

                gfree = gbuf_free[kc % 2]

                def geo(p, kc=kc, w=w, isdiag=isdiag, gfree=gfree):
                    p.wait_ge(sMini.h, sMini.n)
                    if kc == 0:
                        p.wait_ge(sKPEV.h, sKPEV.n)
                    if gfree > 0:
                        p.wait_ge(sEXG.h, gfree)
                    ins = p.tensor_scalar(out=g.gbuf[kc % 2][:, 0:w],
                                          in0=g.kp_b[:, kc * 512:kc * 512 + w],
                                          scalar1=g.bmini[:, 0:1], scalar2=None,
                                          op0=ALU.mult)
                    if isdiag:
                        ins = p.tensor_add(g.gbuf[kc % 2][:, w - 128:w],
                                           g.gbuf[kc % 2][:, w - 128:w], g.c_mask[:, :])
                    sGeo.inc(ins)
                block.vector(geo)

                def expg(s, kc=kc, w=w):
                    s.wait_ge(sGeo.h, sGeo.n)
                    if kc == 0:
                        s.wait_ge(sTPb.h, sTPb.n)
                    sEXG.inc(s.activation(out=g.expG[:, kc * 512:kc * 512 + w],
                                          in_=g.gbuf[kc % 2][:, 0:w], func=AF.Exp,
                                          bias=g.bmini[:, 5:6], scale=1.0,
                                          accum_out=g.accG[:, kc:kc + 1]))
                block.scalar(expg)
                gbuf_free[kc % 2] = sEXG.n

            def sums(v, h=h, ncki=ncki):
                v.wait_ge(sEXP.h, sEXP.n)
                v.wait_ge(sEXG.h, sEXG.n)
                v.reduce_sum(g.bmini[:, 6:7], g.accP[:, 0:ncki], axis=AX.X)
                sE.inc(v.reduce_sum(g.bmini[:, 7:8], g.accG[:, 0:ncki], axis=AX.X))
                v.wait_ge(sE.h, sE.n)
                v.reciprocal(g.bmini[:, 6:7], g.bmini[:, 6:7])
                sE.inc(v.reciprocal(g.bmini[:, 7:8], g.bmini[:, 7:8]))
                v.wait_ge(sE.h, sE.n)
                v.tensor_scalar(out=g.bmini[:, 8:9], in0=g.bmini[:, 6:7],
                                scalar1=g.omscol[:, h:h + 1], scalar2=None, op0=ALU.mult)
                sE.inc(v.tensor_scalar(out=g.bmini[:, 9:10], in0=g.bmini[:, 7:8],
                                       scalar1=g.scol[:, h:h + 1], scalar2=None,
                                       op0=ALU.mult))
                v.wait_ge(sE.h, sE.n)
                sBl1.inc(v.tensor_scalar(out=g.expP[:, 0:Ki], in0=g.expP[:, 0:Ki],
                                         scalar1=g.bmini[:, 8:9], scalar2=None,
                                         op0=ALU.mult))
            block.vector(sums)

            def blend(p, i=i):
                p.wait_ge(sBl1.h, sBl1.n)
                sBl2.inc(p.scalar_tensor_tensor(out=g.expG[:, 0:Ki], in0=g.expG[:, 0:Ki],
                                                scalar=g.bmini[:, 9:10],
                                                in1=g.expP[:, 0:Ki],
                                                op0=ALU.mult, op1=ALU.add))
            block.vector(blend)

            npc = i + 1
            ngrp = (npc + 3) // 4
            baseTG = sTPb.n
            baseEG = sMT.n
            baseA = sAMM.n
            aev_gate = max(0, sAEV.n - 1)

            def tsz(G, npc=npc):
                return min(4 * (G + 1), npc) - 4 * G

            def ta_pe(t, i=i, u=u, npc=npc, ngrp=ngrp, baseTG=baseTG, baseEG=baseEG,
                      aev_gate=aev_gate):
                t.wait_ge(sBl2.h, sBl2.n)
                if baseEG > 0:
                    t.wait_ge(sMT.h, baseEG)
                if aev_gate > 0:
                    t.wait_ge(sAEV.h, aev_gate)

                def emit_tg(G):
                    if G >= 2:
                        t.wait_ge(sMT.h, baseEG + G - 1)
                    pt = g.pt_b[G % 2]
                    ins = None
                    for j in range(tsz(G)):
                        p = 4 * G + j
                        ins = t.transpose(pt[:, j * 128:(j + 1) * 128],
                                          g.expG[:, p * 128:(p + 1) * 128],
                                          g.c_idbf[:, :])
                    sTPb.inc(ins)

                def emit_ag(G):
                    t.wait_ge(sMT.h, baseEG + G + 1)
                    for j in range(tsz(G)):
                        p = 4 * G + j
                        sAMM.inc(t.matmul(g.pa[u % 2][:, 0:256],
                                          g.mt[:, ((p % 8)) * 128:((p % 8) + 1) * 128],
                                          g.vsb[:, p * 256:(p + 1) * 256],
                                          start=(p == 0), stop=(p == npc - 1)))

                emit_tg(0)
                for G in range(1, ngrp):
                    emit_tg(G)
                    emit_ag(G - 1)
                emit_ag(ngrp - 1)
            block.tensor(ta_pe)

            def mt_ev(v, npc=npc, ngrp=ngrp, baseTG=baseTG, baseEG=baseEG, baseA=baseA):
                for G in range(ngrp):
                    v.wait_ge(sTPb.h, baseTG + G + 1)
                    if G >= 2:
                        v.wait_ge(sAMM.h, baseA + min(4 * (G - 1), npc))
                    elif G == 0 and baseA > 0:
                        v.wait_ge(sAMM.h, baseA)
                    sz = tsz(G, npc)
                    pt = g.pt_b[G % 2]
                    sMT.inc(v.tensor_copy(
                        out=g.mt[:, (G % 2) * 512:(G % 2) * 512 + sz * 128],
                        in_=pt[:, 0:sz * 128]))
            block.vector(mt_ev)

            def a_ev(s, u=u):
                s.wait_ge(sAMM.h, sAMM.n)
                if sAO.n >= 32:
                    s.wait_ge(sAO.h, sAO.n - 16)
                sAEV.inc(s.activation(out=g.astage[u % 2][:, :], in_=g.pa[u % 2][:, 0:256],
                                      func=AF.Identity))
            block.scalar(a_ev)

            def a_out(q, h=h, i=i, u=u):
                q.wait_ge(sAEV.h, sAEV.n)
                sAO.dinc(q.dma_start(out=g.ag_in.ap()[i * 128:(i + 1) * 128,
                                                      h * 256:(h + 1) * 256],
                                     in_=g.astage[u % 2][:, :]))
            block.scalar(a_out)

            unit_qfree.append(sSC.n)


def emit_collective(g):
    S, block = g.S, g.block

    def cc(p):
        p.wait_ge(S("sB_ao").h, S("sB_ao").n)
        S("s_cc").inc(p.collective_compute(
            "AllGather", ALU.bypass,
            replica_groups=[[0, 1, 2, 3], [4, 5, 6, 7]],
            ins=[g.ag_in.ap().opt()],
            outs=[g.ag_out.ap().opt()],
        ))
        p.wait_ge(S("s_cc").h, S("s_cc").n)
        if g.dbg:
            S("s_dbg").dinc(p.dma_start(out=g.dbg_agin.ap()[:, :], in_=g.ag_in.ap()[:, :]))
            S("s_dbg").dinc(p.dma_start(out=g.dbg_agout.ap()[:, :], in_=g.ag_out.ap()[:, :]))
            p.wait_ge(S("s_dbg").h, S("s_dbg").n)
    block.gpsimd(cc)


def emit_phaseC(g):
    nc, S, block, sb, ps = g.nc, g.S, g.block, g.sb, g.ps
    g.es_b.close()
    g.es_ab.close()

    sE, sEA = S("sC_e"), S("sC_ea")

    sG1, sGATE = S("sC_g1"), S("sC_gate")
    for ns in range(16):
        def g_d1(v, ns=ns):
            if ns == 0:
                v.wait_ge(S("sA_log").h, S("sA_log").n)
            if sEA.n > 0:
                v.wait_ge(sEA.h, sEA.n)
            sG1.inc(v.reduce_max(g.cmini[:, 0:1], g.logits[:, ns * 16:ns * 16 + 8],
                                 axis=AX.X, negate=True))
        block.vector(g_d1)

        def g_a1(s, ns=ns):
            s.wait_ge(sG1.h, sG1.n)
            sEA.inc(s.activation(out=g.probs[:, :], in_=g.logits[:, ns * 16:ns * 16 + 8],
                                 func=AF.Exp, bias=g.cmini[:, 0:1], scale=1.0,
                                 accum_out=g.cmini[:, 1:2]))
        block.scalar(g_a1)

        def g_d2(v, ns=ns):
            v.wait_ge(sEA.h, sEA.n)
            sE.inc(v.reciprocal(g.cmini[:, 2:3], g.cmini[:, 1:2]))
            v.wait_ge(sE.h, sE.n)
            sE.inc(v.tensor_scalar(out=g.probs[:, :], in0=g.probs[:, :],
                                   scalar1=g.cmini[:, 2:3], scalar2=None, op0=ALU.mult))
            v.wait_ge(sE.h, sE.n)
            sE.inc(v.reduce_max(g.cmini[:, 3:4], g.probs[:, :], axis=AX.X))
            v.wait_ge(sE.h, sE.n)
            sE.inc(v.tensor_scalar(out=g.pmask[:, :], in0=g.probs[:, :],
                                   scalar1=g.cmini[:, 3:4], scalar2=None, op0=ALU.is_ge))
            v.wait_ge(sE.h, sE.n)
            sE.inc(v.scalar_tensor_tensor(out=g.pmask[:, :], in0=g.pmask[:, :],
                                          scalar=-1e30, in1=g.probs[:, :],
                                          op0=ALU.mult, op1=ALU.add))
            v.wait_ge(sE.h, sE.n)
            sE.inc(v.reduce_max(g.cmini[:, 4:5], g.pmask[:, :], axis=AX.X))
            v.wait_ge(sE.h, sE.n)
            sE.inc(v.tensor_scalar(out=g.pmask[:, :], in0=g.probs[:, :],
                                   scalar1=g.cmini[:, 4:5], scalar2=None, op0=ALU.is_ge))
            v.wait_ge(sE.h, sE.n)
            sGATE.inc(v.tensor_mul(g.gates[:, ns * 8:(ns + 1) * 8], g.pmask[:, :],
                                   g.probs[:, :]))
        block.vector(g_d2)

        def g_a2(s, ns=ns):
            s.wait_ge(sGATE.h, sGATE.n)
            sEA.inc(s.activation(out=g.sg[:, ns:ns + 1],
                                 in_=g.logits[:, ns * 16 + 8:ns * 16 + 9], func=AF.Sigmoid))
        block.scalar(g_a2)

    def gbar(q):
        q.wait_ge(S("sC_gate").h, S("sC_gate").n)
        q.wait_ge(sEA.h, sEA.n)
    block.scalar(gbar)

    if g.dbg:
        def gdump(q):
            q.wait_ge(S("sC_gate").h, S("sC_gate").n)
            q.wait_ge(sEA.h, sEA.n)
            S("s_dbg").dinc(q.dma_start(out=g.dbg_gates.ap()[:, 0:128], in_=g.gates[:, :]))
            S("s_dbg").dinc(q.dma_start(out=g.dbg_gates.ap()[:, 128:144], in_=g.sg[:, :]))
        block.scalar(gdump)

    es_c2 = ExitStack()
    xn_sb = sb("xn_sb", [128, NCC * 2048], BF16, es_c2)
    fstage = [sb(f"fstage{j}", [128, C], F32, es_c2) for j in range(2)]
    fbf = [sb(f"fbf{j}", [128, C], BF16, es_c2) for j in range(2)]
    h1s = sb("h1s", [128, OWN], BF16, es_c2)
    hst = [sb(f"hst{j}", [128, OWN], BF16, es_c2) for j in range(2)]
    psH = [ps(f"psH{j}", [128, 512], F32, es_c2) for j in range(8)]

    sXNL, sFS, sFC, sFMM = S("sC_xnl"), S("sC_fs"), S("sC_fc"), S("sC_fmm")
    sSIL, sHTT, sHW = S("sC_sil"), S("sC_htt"), S("sC_hw")

    def xnl(q):
        q.wait_ge(S("sA_qw").h, S("sA_qw").n)
        q.wait_ge(S("sB_amm").h, S("sB_amm").n)
        q.wait_ge(S("sB_mt").h, S("sB_mt").n)
        q.wait_ge(S("sB_aev").h, S("sB_aev").n)
        q.wait_ge(S("sB_ao").h, S("sB_ao").n)
        for cc in range(NCC):
            sXNL.dinc(q.dma_start(out=xn_sb[:, cc * 2048:(cc + 1) * 2048],
                                  in_=g.xn_scr.ap()[cc, :, :]))
    block.sync(xnl)

    strip_jobs = []
    for e in range(E):
        for f in range(NFM):
            strip_jobs.append((g.fc1_t, e, f, 0))
            strip_jobs.append((g.fc2_t, e, f, 1))
    n_moe_strips = len(strip_jobs)
    for f in range(NFS):
        strip_jobs.append((g.sfc1_t, None, f, 0))
        strip_jobs.append((g.sfc2_t, None, f, 1))

    for wj, (src, e, f, layer) in enumerate(strip_jobs):
        shared = e is None

        def fdma(q, wj=wj, src=src, e=e, f=f, shared=shared):
            if wj >= 2:
                q.wait_ge(sFC.h, wj - 1)
            if shared:
                ap = src[:, f * 128:(f + 1) * 128].rearrange("(cc p) j -> p cc j", p=128)
            else:
                ap = src[e, :, f * 128:(f + 1) * 128].rearrange("(cc p) j -> p cc j", p=128)
            dst = fstage[wj % 2][:, :].rearrange("p (cc j) -> p cc j", cc=NCC)
            sFS.dinc(q.dma_start(out=dst, in_=ap))
        block.sync(fdma)

        def fcast(p, wj=wj):
            p.wait_ge(sFS.h, (wj + 1) * 16)
            if wj == 0:
                p.wait_ge(S("sB_amm").h, S("sB_amm").n)
                p.wait_ge(S("sB_ao").h, S("sB_ao").n)
            if wj >= 2:
                p.wait_ge(sFMM.h, wj - 1)
            sFC.inc(p.tensor_copy(out=fbf[wj % 2][:, :], in_=fstage[wj % 2][:, :]))
        block.gpsimd(fcast)

        def fmm(t, wj=wj, layer=layer):
            t.wait_ge(sFC.h, wj + 1)
            if wj == 0:
                t.wait_ge(sXNL.h, sXNL.n)
                t.wait_ge(S("sB_mt").h, S("sB_mt").n)
                t.wait_ge(S("sB_aev").h, S("sB_aev").n)
            if layer == 0:
                if sSIL.n > 0:
                    t.wait_ge(sSIL.h, sSIL.n)
            else:
                if sHTT.n > 0:
                    t.wait_ge(sHTT.h, sHTT.n)
            boff = 0 if layer == 0 else 4
            for cc in range(NCC):
                for nsg in range(4):
                    ins = t.matmul(psH[boff + nsg][:, :],
                                   fbf[wj % 2][:, cc * 128:(cc + 1) * 128],
                                   xn_sb[:, cc * 2048 + nsg * 512: cc * 2048 + nsg * 512 + 512],
                                   start=(cc == 0), stop=(cc == NCC - 1))
                    if cc == NCC - 1 and nsg == 3:
                        sFMM.inc(ins)
        block.tensor(fmm)

        if layer == 0:
            def silu(s, wj=wj):
                s.wait_ge(sFMM.h, wj + 1)
                if sHTT.n > 0:
                    s.wait_ge(sHTT.h, sHTT.n)
                for nsg in range(4):
                    ins = s.activation(out=h1s[:, nsg * 512:(nsg + 1) * 512],
                                       in_=psH[nsg][:, :], func=AF.Silu)
                sSIL.inc(ins)
            block.scalar(silu)
        else:
            hidx = wj // 2

            def htt(v, wj=wj, hidx=hidx):
                v.wait_ge(sFMM.h, wj + 1)
                v.wait_ge(sSIL.h, sSIL.n)
                if sHW.n >= 32:
                    v.wait_ge(sHW.h, sHW.n - 16)
                for nsg in range(4):
                    ins = v.tensor_mul(hst[hidx % 2][:, nsg * 512:(nsg + 1) * 512],
                                       h1s[:, nsg * 512:(nsg + 1) * 512],
                                       psH[4 + nsg][:, :])
                sHTT.inc(ins)
            block.vector(htt)

            def hw(q, e=e, f=f, hidx=hidx, shared=shared):
                q.wait_ge(sHTT.h, sHTT.n)
                if shared:
                    dst = g.sh_scr.ap()[f, :, :]
                else:
                    dst = g.h_scr.ap()[e, f, :, :]
                sHW.dinc(q.dma_start(out=dst, in_=hst[hidx % 2][:, :]))
            block.scalar(hw)

    es_c2.close()

    es_c4 = ExitStack()
    y_sb = sb("y_sb", [128, 16 * C], F32, es_c4)
    pstage = sb("pstage", [128, C], F32, es_c4)
    es_c4a = ExitStack()
    hstrip = [sb(f"hstrip{j}", [128, OWN], BF16, es_c4a) for j in range(6)]
    pbf = [sb(f"pbf{j}", [128, C], BF16, es_c4a) for j in range(6)]
    psY = [ps(f"psY{j}", [128, 512], F32, es_c4a) for j in range(8)]

    sYL, sHS, sPS, sPC, sYM, sYA = (S("sC_yl"), S("sC_hs"), S("sC_ps"), S("sC_pc"),
                                    S("sC_ym"), S("sC_ya"))

    def yload(q):
        q.wait_ge(S("sC_fmm").h, S("sC_fmm").n)
        q.wait_ge(S("sC_htt").h, S("sC_htt").n)
        q.wait_ge(S("sC_hw").h, S("sC_hw").n)
        q.wait_ge(S("sC_sil").h, S("sC_sil").n)
        for ns in range(16):
            sYL.dinc(q.dma_start(out=y_sb[:, ns * C:(ns + 1) * C],
                                 in_=g.x_own[ns * 128:(ns + 1) * 128, :]))
    block.sync(yload)

    for e in (range(E) if "c4" in g.feat else []):
        for (f0, f1) in [(0, 6), (6, 11)]:
            nf = f1 - f0
            ya_gate = sYA.n
            pc_base = sPC.n

            def hsl(q, e=e, f0=f0, f1=f1, ya_gate=ya_gate, pc_base=pc_base):
                q.wait_ge(S("sC_hw").h, S("sC_hw").n)
                if ya_gate > 0:
                    q.wait_ge(sYA.h, ya_gate)
                for f in range(f0, f1):
                    sHS.dinc(q.dma_start(out=hstrip[f - f0][:, :],
                                         in_=g.h_scr.ap()[e, f, :, :]))
                for j, f in enumerate(range(f0, f1)):
                    if pc_base + j > 0:
                        q.wait_ge(sPC.h, pc_base + j)
                    sPS.dinc(q.dma_start(out=pstage[:, :],
                                         in_=g.proj_t[e, f * 128:(f + 1) * 128, :]))
            block.sync(hsl)

            ps_base = sPS.n - nf * 16

            def pcast(p, nf=nf, ps_base=ps_base, ya_gate=ya_gate):
                for j in range(nf):
                    p.wait_ge(sPS.h, ps_base + (j + 1) * 16)
                    if ya_gate > 0 and j == 0:
                        p.wait_ge(sYA.h, ya_gate)
                    sPC.inc(p.tensor_copy(out=pbf[j][:, :], in_=pstage[:, :]))
            block.gpsimd(pcast)

            ya_base = sYA.n
            ym_base = sYM.n

            def ymm(t, e=e, f0=f0, nf=nf, ya_base=ya_base):
                t.wait_ge(sPC.h, sPC.n)
                t.wait_ge(sHS.h, sHS.n)
                if ya_base == 0:
                    t.wait_ge(S("sC_sil").h, S("sC_sil").n)
                    t.wait_ge(S("sC_htt").h, S("sC_htt").n)
                else:
                    t.wait_ge(sYA.h, ya_base)
                for ns in range(16):
                    if ns >= 2:
                        t.wait_ge(sYA.h, ya_base + (ns - 1) * 4)
                    for j in range(nf):
                        for ct in range(4):
                            ins = t.matmul(psY[(ns % 2) * 4 + ct][:, :],
                                           hstrip[j][:, ns * 128:(ns + 1) * 128],
                                           pbf[j][:, ct * 512:(ct + 1) * 512],
                                           start=(j == 0), stop=(j == nf - 1))
                            if j == nf - 1:
                                sYM.inc(ins)
            block.tensor(ymm)

            def yadd(v, e=e, ym_base=ym_base):
                for ns in range(16):
                    v.wait_ge(sYM.h, ym_base + (ns + 1) * 4)
                    if ns == 0:
                        v.wait_ge(sYL.h, sYL.n)
                        v.wait_ge(S("sC_gate").h, S("sC_gate").n)
                    for ct in range(4):
                        sYA.inc(v.scalar_tensor_tensor(
                            out=y_sb[:, ns * C + ct * 512: ns * C + ct * 512 + 512],
                            in0=psY[(ns % 2) * 4 + ct][:, :],
                            scalar=g.gates[:, ns * 8 + e: ns * 8 + e + 1],
                            in1=y_sb[:, ns * C + ct * 512: ns * C + ct * 512 + 512],
                            op0=ALU.mult, op1=ALU.add))
            block.vector(yadd)

    es_c4a.close()

    es_c5 = ExitStack()
    shstrip = [sb(f"shstrip{j}", [128, OWN], BF16, es_c5) for j in range(11)]
    sptile = sb("sptile", [128, 11 * 128], F32, es_c5)
    spbf = sb("spbf", [128, 11 * 128], BF16, es_c5)
    shev = sb("shev", [128, 512], F32, es_c5)
    psS5 = ps("psS5", [128, 512], F32, es_c5)
    pt32 = ps("pt32", [128, 512], F32, es_c5)

    sSHS, sSPS, sSPC, sSM, sSEV, sST2, sSA = (S("sC_shs"), S("sC_sps"), S("sC_spc"),
                                              S("sC_sm"), S("sC_sev"), S("sC_st2"),
                                              S("sC_sa"))

    for fq in (range(4) if "c5" in g.feat else []):
        def shsl(q, fq=fq):
            q.wait_ge(S("sC_hw").h, S("sC_hw").n)
            if fq == 0:
                q.wait_ge(S("sC_ya").h, S("sC_ya").n)
            if sSA.n > 0:
                q.wait_ge(sSA.h, sSA.n)
            for j in range(11):
                sSHS.dinc(q.dma_start(out=shstrip[j][:, :],
                                      in_=g.sh_scr.ap()[fq * 11 + j, :, :]))
        block.sync(shsl)

        for cc in range(NCC):
            def spl(q, fq=fq, cc=cc):
                if fq == 0 and cc == 0:
                    q.wait_ge(S("sC_ya").h, S("sC_ya").n)
                if sSPC.n > 0:
                    q.wait_ge(sSPC.h, sSPC.n)
                for j in range(11):
                    f = fq * 11 + j
                    sSPS.dinc(q.dma_start(out=sptile[:, j * 128:(j + 1) * 128],
                                          in_=g.sproj_t[f * 128:(f + 1) * 128,
                                                        cc * 128:(cc + 1) * 128]))
            block.sync(spl)

            def spc(p, fq=fq, cc=cc):
                p.wait_ge(sSPS.h, sSPS.n)
                if sSM.n > 0:
                    p.wait_ge(sSM.h, sSM.n)
                sSPC.inc(p.tensor_copy(out=spbf[:, :], in_=sptile[:, :]))
            block.gpsimd(spc)

            for nsT in range(4):
                def spmm(t, fq=fq, cc=cc, nsT=nsT):
                    t.wait_ge(sSPC.h, sSPC.n)
                    t.wait_ge(sSHS.h, sSHS.n)
                    if fq == 0 and cc == 0 and nsT == 0:
                        t.wait_ge(S("sC_ya").h, S("sC_ya").n)
                    if sSEV.n > 0:
                        t.wait_ge(sSEV.h, sSEV.n)
                    for j in range(11):
                        ins = t.matmul(psS5[:, :], spbf[:, j * 128:(j + 1) * 128],
                                       shstrip[j][:, nsT * 512:(nsT + 1) * 512],
                                       start=(j == 0), stop=(j == 10))
                        if j == 10:
                            sSM.inc(ins)
                block.tensor(spmm)

                def spev(v, fq=fq, cc=cc, nsT=nsT):
                    v.wait_ge(sSM.h, sSM.n)
                    if sST2.n > 0:
                        v.wait_ge(sST2.h, sST2.n)
                    sSEV.inc(v.tensor_copy(out=shev[:, :], in_=psS5[:, :]))
                block.vector(spev)

                def spt(t, fq=fq, cc=cc, nsT=nsT):
                    t.wait_ge(sSEV.h, sSEV.n)
                    if sSA.n > 0:
                        t.wait_ge(sSA.h, sSA.n)
                    for sub in range(4):
                        sST2.inc(t.transpose(pt32[:, sub * 128:(sub + 1) * 128],
                                             shev[:, sub * 128:(sub + 1) * 128],
                                             g.c_idf[:, :]))
                block.tensor(spt)

                def spa(v, fq=fq, cc=cc, nsT=nsT):
                    v.wait_ge(sST2.h, sST2.n)
                    for sub in range(4):
                        ns = nsT * 4 + sub
                        sSA.inc(v.scalar_tensor_tensor(
                            out=y_sb[:, ns * C + cc * 128: ns * C + cc * 128 + 128],
                            in0=pt32[:, sub * 128:(sub + 1) * 128],
                            scalar=g.sg[:, ns:ns + 1],
                            in1=y_sb[:, ns * C + cc * 128: ns * C + cc * 128 + 128],
                            op0=ALU.mult, op1=ALU.add))
                block.vector(spa)

    es_c5.close()

    es_c6 = ExitStack()
    wo_sb = sb("wo_sb", [128, NCC * 1024], BF16, es_c6)
    agst = [sb(f"agst{j}", [128, 512], BF16, es_c6) for j in range(2)]
    attnT = sb("attnT", [128, NCC * 128], BF16, es_c6)
    psW = [ps(f"psW{j}", [128, 512], F32, es_c6) for j in range(2)]
    ptb6 = ps("ptb6", [128, 512], BF16, es_c6)

    sWOS, sWOC = S("sC_wos"), S("sC_woc")
    sAGG, sAT, sATE, sWM, sWA = S("sC_agg"), S("sC_at"), S("sC_ate"), S("sC_wm"), S("sC_wa")
    sOUT = S("s_out")

    for ch2 in (range(2) if "c6" in g.feat else []):
        woc_base = sWOC.n

        def wol(q, ch2=ch2, woc_base=woc_base):
            if ch2 == 0:
                q.wait_ge(S("sC_spc").h, S("sC_spc").n)
            else:
                q.wait_ge(S("sC_wm").h, S("sC_wm").n)
            for cin in range(NCC):
                if woc_base + cin > 0:
                    q.wait_ge(sWOC.h, woc_base + cin)
                sWOS.dinc(q.dma_start(out=pstage[:, 0:1024],
                                      in_=g.wo[cin * 128:(cin + 1) * 128,
                                               ch2 * 1024:(ch2 + 1) * 1024]))
        block.sync(wol)

        wos_base = sWOS.n - NCC * 16

        def woc(p, ch2=ch2, wos_base=wos_base):
            if ch2 == 0:
                p.wait_ge(S("sC_sm").h, S("sC_sm").n)
                p.wait_ge(S("sC_sa").h, S("sC_sa").n)
            for cin in range(NCC):
                p.wait_ge(sWOS.h, wos_base + (cin + 1) * 16)
                sWOC.inc(p.tensor_copy(out=wo_sb[:, cin * 1024:(cin + 1) * 1024],
                                       in_=pstage[:, 0:1024]))
        block.gpsimd(woc)

        for ns in range(16):
            for r in range(4):
                ag_no = (ch2 * 16 + ns) * 4 + r

                def agg(p, ns=ns, r=r, ag_no=ag_no):
                    if ag_no == 0:
                        p.wait_ge(S("s_cc").h, S("s_cc").n)
                        p.wait_ge(S("sC_sm").h, S("sC_sm").n)
                        p.wait_ge(S("sC_sa").h, S("sC_sa").n)
                    if ag_no >= 2:
                        p.wait_ge(sAT.h, (ag_no - 1) * 4)
                    sAGG.dinc(p.indirect_dma_start(
                        out=agst[ag_no % 2][:, :], out_offset=None,
                        in_=g.ag_out.ap()[0:128, :],
                        in_offset=IndirectOffsetOnAxis(
                            ap=g.ridx[:, r * 16 + ns: r * 16 + ns + 1], axis=0)))
                block.gpsimd(agg)

                def att(t, ns=ns, r=r, ag_no=ag_no):
                    t.wait_ge(sAGG.h, sAGG.n)
                    if sATE.n > 0:
                        t.wait_ge(sATE.h, sATE.n)
                    if ag_no == 0 and S("sC_sa").n > 0:
                        t.wait_ge(S("sC_sa").h, S("sC_sa").n)
                    for sub in range(4):
                        sAT.inc(t.transpose(ptb6[:, sub * 128:(sub + 1) * 128],
                                            agst[ag_no % 2][:, sub * 128:(sub + 1) * 128],
                                            g.c_idbf[:, :]))
                block.tensor(att)

                def ate(v, ns=ns, r=r):
                    v.wait_ge(sAT.h, sAT.n)
                    if r == 0 and sWM.n > 0:
                        v.wait_ge(sWM.h, sWM.n)
                    for sub in range(4):
                        cin = r * 4 + sub
                        sATE.inc(v.tensor_copy(out=attnT[:, cin * 128:(cin + 1) * 128],
                                               in_=ptb6[:, sub * 128:(sub + 1) * 128]))
                block.vector(ate)

            def wmm(t, ns=ns, ch2=ch2):
                t.wait_ge(sATE.h, sATE.n)
                t.wait_ge(sWOC.h, sWOC.n)
                if sWA.n > 0:
                    t.wait_ge(sWA.h, sWA.n)
                for cin in range(NCC):
                    for ct in range(2):
                        ins = t.matmul(psW[ct][:, :], attnT[:, cin * 128:(cin + 1) * 128],
                                       wo_sb[:, cin * 1024 + ct * 512: cin * 1024 + ct * 512 + 512],
                                       start=(cin == 0), stop=(cin == NCC - 1))
                        if cin == NCC - 1:
                            sWM.inc(ins)
            block.tensor(wmm)

            def wadd(v, ns=ns, ch2=ch2):
                v.wait_ge(sWM.h, sWM.n)
                if ch2 == 0 and ns == 0 and S("sC_sa").n > 0:
                    v.wait_ge(S("sC_sa").h, S("sC_sa").n)
                for ct in range(2):
                    o = ns * C + ch2 * 1024 + ct * 512
                    sWA.inc(v.tensor_add(y_sb[:, o:o + 512], y_sb[:, o:o + 512],
                                         psW[ct][:, :]))
            block.vector(wadd)

    for ns in range(16):
        def yout(q, ns=ns):
            if ns == 0:
                q.wait_ge(S("sC_ya").h, S("sC_ya").n)
                q.wait_ge(S("sC_sa").h, S("sC_sa").n)
                q.wait_ge(sWA.h, sWA.n)
            sOUT.dinc(q.dma_start(out=g.out[ns * 128:(ns + 1) * 128, :],
                                  in_=y_sb[:, ns * C:(ns + 1) * C]))
        block.scalar(yout)

    def fin(q):
        q.wait_ge(sOUT.h, sOUT.n)
    block.scalar(fin)
    es_c6.close()
    es_c4.close()



_CACHE = {}


def _consts():
    if "consts" in _CACHE:
        return _CACHE["consts"]
    t = np.arange(T, dtype=np.float64)
    inv = (1.0 / (10000.0 ** (np.arange(0, DH, 2, dtype=np.float64) / DH))).astype(np.float32)
    f = (t[:, None] * inv[None, :].astype(np.float64)).astype(np.float32)
    cos_t = np.ascontiguousarray(np.cos(f.astype(np.float64)).astype(np.float32).T)
    sin_t = np.ascontiguousarray(np.sin(f.astype(np.float64)).astype(np.float32).T)
    trig = np.array([[sx, sy, sz] for sx in (1.0, -1.0) for sy in (1.0, -1.0)
                     for sz in (1.0, -1.0)], dtype=np.float32)
    dirs = (trig / np.sqrt(3.0)).astype(np.float32)
    ident_bf = np.eye(128, dtype=BF)
    ident_f = np.eye(128, dtype=np.float32)
    ones_f = np.ones((1, 128), dtype=np.float32)
    ones_bf = np.ones((128, 1), dtype=BF)
    mask = np.triu(np.full((128, 128), NEG, dtype=np.float32), k=1)
    _CACHE["consts"] = (cos_t, sin_t, dirs, ident_bf, ident_f, ones_f, ones_bf, mask)
    return _CACHE["consts"]


def _prepare(L0_attn_x, L0_attn_Wq, L0_attn_Wk, L0_attn_Wv, L0_attn_Wo,
             L0_attn_head_scales, L0_ffn_gate_w, L0_ffn_expert_fc1,
             L0_ffn_expert_fc2, L0_ffn_expert_proj, L0_ffn_shared_fc1,
             L0_ffn_shared_fc2, L0_ffn_shared_proj, L0_ffn_shared_gate_w):
    cos_t, sin_t, dirs, ident_bf, ident_f, ones_f, ones_bf, mask = _consts()
    asc = np.ascontiguousarray

    x_np = np.asarray(L0_attn_x, dtype=np.float32)
    gwsg = np.zeros((C, 16), dtype=np.float32)
    gwsg[:, 0:8] = np.asarray(L0_ffn_gate_w, dtype=np.float32).T
    gwsg[:, 8:9] = np.asarray(L0_ffn_shared_gate_w, dtype=np.float32).T

    fc1_t = asc(np.asarray(L0_ffn_expert_fc1, dtype=np.float32).transpose(0, 2, 1))
    fc2_t = asc(np.asarray(L0_ffn_expert_fc2, dtype=np.float32).transpose(0, 2, 1))
    proj_t = asc(np.asarray(L0_ffn_expert_proj, dtype=np.float32).transpose(0, 2, 1))
    sfc1_t = asc(np.asarray(L0_ffn_shared_fc1, dtype=np.float32).T)
    sfc2_t = asc(np.asarray(L0_ffn_shared_fc2, dtype=np.float32).T)
    sproj_t = asc(np.asarray(L0_ffn_shared_proj, dtype=np.float32).T)
    wq_np = np.asarray(L0_attn_Wq, dtype=np.float32)
    wk_np = np.asarray(L0_attn_Wk, dtype=np.float32)
    wv_np = np.asarray(L0_attn_Wv, dtype=np.float32)
    wo_np = asc(np.asarray(L0_attn_Wo, dtype=np.float32))
    hsc = np.asarray(L0_attn_head_scales, dtype=np.float32)

    actc = np.zeros((128, 2), dtype=np.float32)
    actc[:, 0] = 1e-5
    actc[:, 1] = 1e-30

    in_maps = []
    for c in range(8):
        b, gi = c // 4, c % 4
        h1 = 2 * gi
        kv = gi // 2
        base = gi * OWN + np.arange(OWN, dtype=np.int32).reshape(16, 128)
        ridx = asc(np.concatenate([(r * T + base) for r in range(4)], axis=0).T)
        in_maps.append({
            "x": asc(x_np[b]),
            "x_own": asc(x_np[b, gi * OWN:(gi + 1) * OWN]),
            "wq": asc(wq_np[:, h1 * DH:(h1 + 2) * DH]),
            "wk": asc(wk_np[:, kv * DH:(kv + 1) * DH]),
            "wv": asc(wv_np[:, kv * DH:(kv + 1) * DH]),
            "wo": wo_np,
            "hs": asc(hsc[h1:h1 + 2].reshape(1, 2)),
            "gwsg_t": gwsg,
            "fc1_t": fc1_t, "fc2_t": fc2_t, "proj_t": proj_t,
            "sfc1_t": sfc1_t, "sfc2_t": sfc2_t, "sproj_t": sproj_t,
            "cos_t": cos_t, "sin_t": sin_t,
            "dirs_c": asc(dirs[h1:h1 + 2].T).astype(BF),
            "dirs_b": asc(np.repeat(dirs[h1:h1 + 2], 128, axis=0).reshape(2, 128, 3)
                          .transpose(2, 0, 1).reshape(3, 256)).astype(BF),
            "ident_bf": ident_bf, "ident_f": ident_f,
            "ones_f": ones_f, "ones_bf": ones_bf, "mask_f": mask,
            "rowidx": ridx,
            "actc": actc,
        })
    return in_maps


def kernel(**inputs):
    in_maps = _prepare(**inputs)
    if "nc" not in _CACHE:
        _CACHE["nc"] = build_nc()
    nc = _CACHE["nc"]
    res = run_bass_kernel_spmd(nc, in_maps, core_ids=list(range(8)))
    outp = np.empty((B, T, C), dtype=np.float32)
    for c in range(8):
        b, gi = c // 4, c % 4
        outp[b, gi * OWN:(gi + 1) * OWN] = res.results[c]["out"]
    return outp
